# version 02
# baseline (speedup 1.0000x reference)
import numpy as np
import ml_dtypes
import concourse.bass as bass
import concourse.mybir as mybir
from concourse.bass_utils import run_bass_kernel_spmd

F32 = mybir.dt.float32
BF16 = mybir.dt.bfloat16
AF = mybir.ActivationFunctionType
ALU = mybir.AluOpType
AX = mybir.AxisListType
BF = ml_dtypes.bfloat16

B, T, D = 4, 4096, 2048
NCORES = 8


class _Stream:
    def __init__(self, kb, name, sems):
        self.kb = kb
        self.name = name
        self.sems = sems
        self.ops = []
        self.known = {}
        self.snapshots = []

class KB:
    ENGINES = ("pe", "dve", "act", "gp", "sync")

    DMA_ROT = 6

    def __init__(self, nc):
        self.nc = nc
        self.streams = {}
        for e in self.ENGINES:
            sems = {e + "_c": 0}
            if e in ("gp", "sync", "act", "dve"):
                for r in range(self.DMA_ROT):
                    sems[f"{e}_d{r}"] = 0
            self.streams[e] = _Stream(self, e, sems)
        self._dma_idx = {e: 0 for e in self.ENGINES}
        self.last_w = {}
        self.readers = {}
        self.nops = 0

    def op(self, eng, fn, reads=(), writes=(), dma=False, label=""):
        st = self.streams[eng]
        if dma:
            r = self._dma_idx[eng] % self.DMA_ROT
            self._dma_idx[eng] += 1
            sem_name = f"{eng}_d{r}"
            inc = 16
        else:
            sem_name = eng + "_c"
            inc = 1
        waits = {}

        def add_wait(sem, val):
            if val is None:
                return
            if st.known.get(sem, 0) >= val:
                return
            if waits.get(sem, 0) < val:
                waits[sem] = val

        if dma and st.sems[sem_name] > 0:
            add_wait(sem_name, st.sems[sem_name])
        for r in reads:
            lw = self.last_w.get(r)
            if lw:
                add_wait(*lw)
        for w in writes:
            lw = self.last_w.get(w)
            if lw:
                add_wait(*lw)
            for rd in self.readers.get(w, ()):
                add_wait(*rd)
        for sem, val in waits.items():
            st.known[sem] = max(st.known.get(sem, 0), val)
            psname = sem.rsplit("_", 1)[0]
            ps = self.streams[psname]
            idx = self._idx_for(ps, sem, val)
            if idx is not None and idx < len(ps.snapshots):
                for s2, v2 in ps.snapshots[idx].items():
                    st.known[s2] = max(st.known.get(s2, 0), v2)

        st.sems[sem_name] += inc
        val_after = st.sems[sem_name]
        st.ops.append((dict(waits), fn, sem_name, inc, label))
        snap = dict(st.known)
        snap[sem_name] = val_after
        st.snapshots.append(snap)
        mark = (sem_name, val_after)
        for w in writes:
            self.last_w[w] = mark
            self.readers[w] = []
        for r in reads:
            self.readers.setdefault(r, []).append(mark)
        self.nops += 1
        return mark

    def constcol(self, nc, value, parts=128):
        cache = self.__dict__.setdefault("_constcols", {})
        key = (value, parts)
        if key not in cache:
            t = nc.alloc_sbuf_tensor(f"cc_{len(cache)}", [parts, 1], F32)
            kk = f"__cc{len(cache)}"
            self.op("dve", lambda e, o=t[:], v=value: e.memset(o, v), writes=(kk,))
            cache[key] = (t, kk)
        return cache[key]

    def _idx_for(self, stream, sem_name, val):
        mc = stream.__dict__.setdefault("_mark_map", {})
        cts = stream.__dict__.setdefault("_mark_counts", {})
        n_done = stream.__dict__.setdefault("_mark_n", 0)
        if n_done < len(stream.ops):
            for i in range(n_done, len(stream.ops)):
                sn, inc = stream.ops[i][2], stream.ops[i][3]
                cts[sn] = cts.get(sn, 0) + inc
                mc[(sn, cts[sn])] = i
            stream._mark_n = len(stream.ops)
        return mc.get((sem_name, val))

    def build_and_emit(self, block):
        semh = {}
        for e in self.ENGINES:
            for sn in self.streams[e].sems:
                semh[sn] = self.nc.alloc_semaphore(name=sn)
        self.semh = semh
        eng_attr = {"pe": "tensor", "dve": "vector", "act": "scalar",
                    "gp": "gpsimd", "sync": "sync"}
        for e in self.ENGINES:
            st = self.streams[e]
            if not st.ops:
                continue
            def make_body(st=st):
                def body(eng):
                    for waits, fn, sem_name, inc, label in st.ops:
                        for sem, val in waits.items():
                            eng.wait_ge(semh[sem], val)
                        ins = fn(eng)
                        if ins is not None:
                            ins.then_inc(semh[sem_name], inc)
                        else:
                            eng.nop().then_inc(semh[sem_name], inc)
                return body
            getattr(block, eng_attr[e])(make_body())

    def finish(self, eng="sync"):
        st = self.streams[eng]
        targets = []
        for e in self.ENGINES:
            s2 = self.streams[e]
            for sn, v in s2.sems.items():
                if "_d" in sn and v > 0:
                    targets.append((sn, v))
        def fn(engobj):
            return None
        waits = {sn: v for sn, v in targets}
        st.sems[eng + "_c"] += 1
        st.ops.append((waits, lambda e: None, eng + "_c", 1, "finish"))
        st.snapshots.append(dict(st.known))


def build_program(emit_fn, name="prog"):
    nc = bass.Bass("TRN2", target_bir_lowering=False, debug=False)
    kb = KB(nc)
    emit_fn(nc, kb)
    kb.finish()
    with nc.Block() as block:
        kb.build_and_emit(block)
    return nc



def hadamard(n, dtype=np.float32):
    H = np.array([[1.0]], dtype=np.float64)
    while H.shape[0] < n:
        H = np.block([[H, H], [H, -H]])
    return H.astype(dtype)

_CONST_CACHE = {}

def consts():
    if _CONST_CACHE:
        return _CONST_CACHE
    c = _CONST_CACHE
    c["H2048_bf"] = (hadamard(2048) * (2.0 ** -5)).astype(BF)
    c["H128_bf"] = (hadamard(128) * (2.0 ** -6)).astype(BF)
    c["H64_bf"] = hadamard(64).astype(BF)
    c["I128_bf"] = np.eye(128, dtype=np.float32).astype(BF)
    c["LONES_f32"] = np.tril(np.ones((128, 128), np.float32))
    c["ONES2_bf"] = None
    return c



def emit_ln(nc, kb, NT, with_residual):
    x = nc.dram_tensor("x", [NT, D], F32, kind="ExternalInput").ap()
    if with_residual:
        a_in = nc.dram_tensor("a", [NT, D], BF16, kind="ExternalInput").ap()
        f_in = nc.dram_tensor("f", [NT, D], BF16, kind="ExternalInput").ap()
        x1_out = nc.dram_tensor("x1", [NT, D], F32, kind="ExternalOutput").ap()
    xn_out = nc.dram_tensor("xn", [NT, D], BF16, kind="ExternalOutput").ap()

    nt = NT // 128
    xt = nc.alloc_sbuf_tensor("xt", [128, 2 * D], F32)
    if with_residual:
        at = nc.alloc_sbuf_tensor("at", [128, 2 * D], BF16)
        ft = nc.alloc_sbuf_tensor("ft", [128, 2 * D], BF16)
    sq = nc.alloc_sbuf_tensor("sq", [128, D], BF16)
    ssum = nc.alloc_sbuf_tensor("ssum", [128, 2], F32)
    ssq = nc.alloc_sbuf_tensor("ssq", [128, 2], F32)
    mu = nc.alloc_sbuf_tensor("mu", [128, 1], F32)
    var = nc.alloc_sbuf_tensor("var", [128, 1], F32)
    rstd = nc.alloc_sbuf_tensor("rstd", [128, 1], F32)
    xnb = nc.alloc_sbuf_tensor("xnb", [128, 2 * D], BF16)

    for i in range(nt):
        pp = i % 2
        xs = xt[:, pp * D:(pp + 1) * D]
        xn_s = xnb[:, pp * D:(pp + 1) * D]
        kx = f"x{pp}"
        kb.op("sync", lambda e, o=xs, s=x[i * 128:(i + 1) * 128, :]: e.dma_start(out=o, in_=s),
              reads=(), writes=(kx,), dma=True)
        if with_residual:
            ka, kf = f"a{pp}", f"f{pp}"
            as_ = at[:, pp * D:(pp + 1) * D]
            fs = ft[:, pp * D:(pp + 1) * D]
            kb.op("sync", lambda e, o=as_, s=a_in[i * 128:(i + 1) * 128, :]: e.dma_start(out=o, in_=s),
                  writes=(ka,), dma=True)
            kb.op("sync", lambda e, o=fs, s=f_in[i * 128:(i + 1) * 128, :]: e.dma_start(out=o, in_=s),
                  writes=(kf,), dma=True)
            kb.op("dve", lambda e, o=xs, i0=xs, i1=as_: e.tensor_tensor(out=o, in0=i0, in1=i1, op=ALU.add),
                  reads=(kx, ka), writes=(kx,))
            kb.op("dve", lambda e, o=xs, i0=xs, i1=fs: e.tensor_tensor(out=o, in0=i0, in1=i1, op=ALU.add),
                  reads=(kx, kf), writes=(kx,))
            kb.op("sync", lambda e, o=x1_out[i * 128:(i + 1) * 128, :], s=xs: e.dma_start(out=o, in_=s),
                  reads=(kx,), dma=True)
        kb.op("dve", lambda e, o=ssum[:, 0:1], s=xs: e.tensor_reduce(out=o, in_=s, axis=AX.X, op=ALU.add),
              reads=(kx,), writes=("ssum",))
        kb.op("act", lambda e, o=sq[:], s=xs, a=ssq[:, 0:1]: e.activation(out=o, in_=s, func=AF.Square, accum_out=a),
              reads=(kx,), writes=("ssq", "sq"))
        kb.op("dve", lambda e, o=mu[:]: e.tensor_scalar(out=o, in0=ssum[:, 0:1], scalar1=1.0 / D, scalar2=None, op0=ALU.mult),
              reads=("ssum",), writes=("mu",))
        kb.op("dve", lambda e, o=var[:]: e.scalar_tensor_tensor(out=o, in0=mu[:], scalar=-1.0, in1=mu[:], op0=ALU.mult, op1=ALU.mult),
              reads=("mu",), writes=("var",))
        kb.op("dve", lambda e, o=var[:]: e.scalar_tensor_tensor(out=o, in0=ssq[:, 0:1], scalar=1.0 / D, in1=var[:], op0=ALU.mult, op1=ALU.add),
              reads=("ssq", "var"), writes=("var",))
        epst, epsk = kb.constcol(nc, 1e-5)
        kb.op("act", lambda e, o=rstd[:], b=epst: e.activation(out=o, in_=var[:], func=AF.Ln, bias=b[:, 0:1], scale=1.0),
              reads=("var", epsk), writes=("rstd",))
        kb.op("act", lambda e, o=rstd[:]: e.activation(out=o, in_=rstd[:], func=AF.Exp, scale=-0.5),
              reads=("rstd",), writes=("rstd",))
        kb.op("dve", lambda e, o=xn_s, s=xs: e.tensor_scalar(out=o, in0=s, scalar1=mu[:, 0:1], scalar2=rstd[:, 0:1], op0=ALU.subtract, op1=ALU.mult),
              reads=(kx, "mu", "rstd"), writes=(f"xn{pp}",))
        kb.op("sync", lambda e, o=xn_out[i * 128:(i + 1) * 128, :], s=xn_s: e.dma_start(out=o, in_=s),
              reads=(f"xn{pp}",), dma=True)


def build_ln(NT, with_residual):
    return build_program(lambda nc, kb: emit_ln(nc, kb, NT, with_residual))



def mm_group(kb, psum_ap, pairs, reads, writes, fp32=False):
    def fn(e, pairs=tuple(pairs), o=psum_ap):
        ins = None
        n = len(pairs)
        for idx, (l, r) in enumerate(pairs):
            ins = e.matmul(o, l, r, start=(idx == 0), stop=(idx == n - 1))
        return ins
    return kb.op("pe", fn, reads=tuple(reads), writes=tuple(writes))



def emit_fold(nc, kb):
    OS = D // NCORES
    h64 = nc.dram_tensor("H64", [64, 64], BF16, kind="ExternalInput").ap()
    h64_sb = nc.alloc_sbuf_tensor("h64sb", [64, 64], BF16)
    kb.op("sync", lambda e: e.dma_start(out=h64_sb[:], in_=h64[:, :]), writes=("h64",), dma=True)
    slab = nc.alloc_sbuf_tensor("slab", [64, 4 * 512], BF16)
    oslab = nc.alloc_sbuf_tensor("oslab", [64, 4 * 512], BF16)
    ps = nc.alloc_psum_tensor("psf", [64, 512], F32)
    for li in range(2):
        wd = nc.dram_tensor(f"Wdn{li}", [OS, 8192], F32, kind="ExternalInput").ap()
        d2 = nc.dram_tensor(f"D2_{li}", [OS, 8192], BF16, kind="ExternalOutput").ap()
        for og in range(OS // 4):
            pp = og % 4
            s_in = slab[:, pp * 512:(pp + 1) * 512]
            s_out = oslab[:, pp * 512:(pp + 1) * 512]
            src = wd[og * 4:(og + 1) * 4, :].rearrange("o (c i) -> c o i", i=128)
            dst = d2[og * 4:(og + 1) * 4, :].rearrange("o (c i) -> c o i", i=128)
            kin, kout = f"sl{pp}", f"so{pp}"
            kb.op("gp", lambda e, o=s_in, s=src: e.dma_start(out=o.rearrange("p (a b) -> p a b", a=4), in_=s),
                  writes=(kin,), dma=True)
            kb.op("pe", lambda e, o=ps[:], l=h64_sb[:], r=s_in: e.matmul(o, l, r, start=True, stop=True),
                  reads=(kin, "h64"), writes=("psf",))
            kb.op("act", lambda e, o=s_out, s=ps[:]: e.activation(out=o, in_=s, func=AF.Copy),
                  reads=("psf",), writes=(kout,))
            kb.op("sync", lambda e, o=dst, s=s_out: e.dma_start(out=o, in_=s.rearrange("p (a b) -> p a b", a=4)),
                  reads=(kout,), dma=True)


def build_fold():
    return build_program(emit_fold)



def emit_l0proj(nc, kb):
    TT = T
    xnf = nc.dram_tensor("xnf", [D, TT], BF16, kind="ExternalInput").ap()
    wqkvT = nc.dram_tensor("wqkvT", [D, 1024], F32, kind="ExternalInput").ap()
    wbT = nc.dram_tensor("wbT", [D, 8], F32, kind="ExternalInput").ap()
    waT = nc.dram_tensor("waT", [D, 8], F32, kind="ExternalInput").ap()
    convw = nc.dram_tensor("convw", [1024, 4], F32, kind="ExternalInput").ap()
    dtb = nc.dram_tensor("dtb", [8, 1], F32, kind="ExternalInput").ap()
    alog = nc.dram_tensor("alog", [8, 1], F32, kind="ExternalInput").ap()
    ones2 = nc.dram_tensor("ones2", [128, 2], BF16, kind="ExternalInput").ap()

    qkvf = nc.dram_tensor("qkvf", [1024, TT], BF16, kind="ExternalOutput").ap()
    grow = nc.dram_tensor("grow", [8, TT], F32, kind="ExternalOutput").ap()
    betarow = nc.dram_tensor("betarow", [8, TT], F32, kind="ExternalOutput").ap()
    lnbrow = nc.dram_tensor("lnbrow", [8, TT], F32, kind="ExternalOutput").ap()
    lnrkrow = nc.dram_tensor("lnrkrow", [4, TT], F32, kind="ExternalOutput").ap()
    lnrqrow = nc.dram_tensor("lnrqrow", [4, TT], F32, kind="ExternalOutput").ap()

    HT = TT // 2
    xn_sb = nc.alloc_sbuf_tensor("xn_sb", [128, 16 * HT], BF16)
    wq_sb = nc.alloc_sbuf_tensor("wq_sb", [128, 16 * 1024], BF16)
    wba_sb = nc.alloc_sbuf_tensor("wba_sb", [128, 16 * 16], BF16)
    qp_sb = nc.alloc_sbuf_tensor("qp_sb", [128, 8 * (HT + 3)], BF16)
    conv_sb = nc.alloc_sbuf_tensor("conv_sb", [128, HT], F32)
    qkv_sb = nc.alloc_sbuf_tensor("qkv_sb", [128, HT], BF16)
    sq_sb = nc.alloc_sbuf_tensor("sq_sb", [128, HT], BF16)
    wc_sb = nc.alloc_sbuf_tensor("wc_sb", [128, 8 * 4], F32)
    o2_sb = nc.alloc_sbuf_tensor("o2_sb", [128, 2], BF16)
    zrow_sb = nc.alloc_sbuf_tensor("zrow_sb", [8, 2 * HT], F32)
    t1_sb = nc.alloc_sbuf_tensor("t1_sb", [8, HT], F32)
    t3_sb = nc.alloc_sbuf_tensor("t3_sb", [8, HT], F32)
    t4_sb = nc.alloc_sbuf_tensor("t4_sb", [8, HT], F32)
    gout_sb = nc.alloc_sbuf_tensor("gout_sb", [8, HT], F32)
    ln_sb = nc.alloc_sbuf_tensor("ln_sb", [2, HT], F32)
    alogc = nc.alloc_sbuf_tensor("alogc", [8, 1], F32)
    dtbc = nc.alloc_sbuf_tensor("dtbc", [8, 1], F32)
    nexp_sb = nc.alloc_sbuf_tensor("nexp_sb", [8, 1], F32)

    psA = nc.alloc_psum_tensor("psA", [128, 512], F32)
    psB = nc.alloc_psum_tensor("psB", [128, 512], F32)
    psS = nc.alloc_psum_tensor("psS", [2, 2048], F32)
    psZ = nc.alloc_psum_tensor("psZ", [8, 512], F32)

    one_c, one_k = kb.constcol(nc, 1.0, parts=8)
    eps_c, eps_k = kb.constcol(nc, 1e-6, parts=2)

    kb.op("gp", lambda e: e.dma_start(out=wq_sb[:].rearrange("p (k m) -> p k m", k=16),
                                      in_=wqkvT.rearrange("(k p) m -> p k m", p=128)),
          writes=("wq",), dma=True)
    kb.op("gp", lambda e: e.dma_start(out=wba_sb[:].rearrange("p (k m) -> p k m", k=16)[:, :, 0:8],
                                      in_=wbT.rearrange("(k p) m -> p k m", p=128)),
          writes=("wba_b",), dma=True)
    kb.op("gp", lambda e: e.dma_start(out=wba_sb[:].rearrange("p (k m) -> p k m", k=16)[:, :, 8:16],
                                      in_=waT.rearrange("(k p) m -> p k m", p=128)),
          writes=("wba_a",), dma=True)
    kb.op("sync", lambda e: e.dma_start(out=wc_sb[:].rearrange("p (c f) -> p c f", c=8),
                                        in_=convw.rearrange("(c p) f -> p c f", p=128)),
          writes=("wc",), dma=True)
    kb.op("sync", lambda e: e.dma_start(out=o2_sb[:], in_=ones2[:, :]), writes=("o2",), dma=True)
    kb.op("sync", lambda e: e.dma_start(out=alogc[:], in_=alog[:, :]), writes=("alog",), dma=True)
    kb.op("sync", lambda e: e.dma_start(out=dtbc[:], in_=dtb[:, :]), writes=("dtb",), dma=True)
    kb.op("act", lambda e: e.activation(out=nexp_sb[:], in_=alogc[:], func=AF.Exp),
          reads=("alog",), writes=("nexp",))
    kb.op("dve", lambda e: e.tensor_scalar(out=nexp_sb[:], in0=nexp_sb[:], scalar1=-1.0, scalar2=None, op0=ALU.mult),
          reads=("nexp",), writes=("nexp",))
    kb.op("dve", lambda e: e.memset(qp_sb[:].rearrange("p (c f) -> p c f", c=8)[:, :, 0:3], 0.0),
          writes=("halo",))

    for half in range(2):
        toff = half * HT
        kb.op("sync", lambda e, t=toff: e.dma_start(
            out=xn_sb[:].rearrange("p (k t) -> p k t", k=16),
            in_=xnf.rearrange("(k p) t -> p k t", p=128)[:, :, t:t + HT]),
              writes=("xn",), dma=True)
        for m in range(8):
            qp_chunk = qp_sb[:, m * (HT + 3):(m + 1) * (HT + 3)]
            for ts_ in range(HT // 512):
                ps = psA if (ts_ % 2 == 0) else psB
                psk = "psA" if (ts_ % 2 == 0) else "psB"
                pairs = [(wq_sb[:, k * 1024 + m * 128:(k * 1024 + (m + 1) * 128)],
                          xn_sb[:, k * HT + ts_ * 512: k * HT + (ts_ + 1) * 512])
                         for k in range(16)]
                mm_group(kb, ps[:], pairs, reads=("wq", "xn"), writes=(psk,))
                kb.op("act", lambda e, o=qp_chunk[:, 3 + ts_ * 512: 3 + (ts_ + 1) * 512], s=ps[:]:
                      e.activation(out=o, in_=s, func=AF.Copy),
                      reads=(psk,), writes=(f"qp{m}",))
            kb.op("dve", lambda e, m=m, qp=qp_chunk: e.tensor_scalar(
                out=conv_sb[:], in0=qp[:, 0:HT], scalar1=wc_sb[:, m * 4:m * 4 + 1], scalar2=None, op0=ALU.mult),
                  reads=(f"qp{m}", "wc"), writes=("conv",))
            for tau in range(1, 4):
                kb.op("dve", lambda e, m=m, tau=tau, qp=qp_chunk: e.scalar_tensor_tensor(
                    out=conv_sb[:], in0=qp[:, tau:tau + HT], scalar=wc_sb[:, m * 4 + tau:m * 4 + tau + 1],
                    in1=conv_sb[:], op0=ALU.mult, op1=ALU.add),
                      reads=(f"qp{m}", "wc", "conv"), writes=("conv",))
            kb.op("act", lambda e: e.activation(out=qkv_sb[:], in_=conv_sb[:], func=AF.Silu),
                  reads=("conv",), writes=("qkv",))
            kb.op("sync", lambda e, m=m, t=toff: e.dma_start(out=qkvf[m * 128:(m + 1) * 128, t:t + HT], in_=qkv_sb[:]),
                  reads=("qkv",), dma=True)
            if half == 0:
                kb.op("dve", lambda e, qp=qp_chunk: e.tensor_copy(qp[:, 0:3], qp[:, HT:HT + 3]),
                      reads=(f"qp{m}",), writes=(f"qp{m}",))
            if m < 4:
                kb.op("act", lambda e: e.activation(out=sq_sb[:], in_=qkv_sb[:], func=AF.Square),
                      reads=("qkv",), writes=("sq",))
                for ts_ in range(HT // 512):
                    kb.op("pe", lambda e, o=psS[:, ts_ * 512:(ts_ + 1) * 512], l=o2_sb[:], r=sq_sb[:, ts_ * 512:(ts_ + 1) * 512]:
                          e.matmul(o, l, r, start=True, stop=True),
                          reads=("sq", "o2"), writes=("psS",))
                kb.op("act", lambda e: e.activation(out=ln_sb[:], in_=psS[0:2, 0:HT], func=AF.Ln, bias=eps_c[:, 0:1]),
                      reads=("psS", eps_k), writes=("lnv",))
                if m < 2:
                    kb.op("dve", lambda e: e.tensor_scalar(out=ln_sb[:], in0=ln_sb[:], scalar1=-0.5,
                                                           scalar2=float(-0.5 * np.log(64.0)), op0=ALU.mult, op1=ALU.add),
                          reads=("lnv",), writes=("lnv",))
                    dst = lnrqrow
                else:
                    kb.op("dve", lambda e: e.tensor_scalar(out=ln_sb[:], in0=ln_sb[:], scalar1=-0.5, scalar2=None, op0=ALU.mult),
                          reads=("lnv",), writes=("lnv",))
                    dst = lnrkrow
                r0 = (m % 2) * 2
                kb.op("sync", lambda e, d=dst, r0=r0, t=toff: e.dma_start(out=d[r0:r0 + 2, t:t + HT], in_=ln_sb[:]),
                      reads=("lnv",), dma=True)
        for which in range(2):
            zrow = zrow_sb[:, which * HT:(which + 1) * HT]
            for ts_ in range(HT // 512):
                pairs = [(wba_sb[:, k * 16 + which * 8: k * 16 + which * 8 + 8],
                          xn_sb[:, k * HT + ts_ * 512: k * HT + (ts_ + 1) * 512])
                         for k in range(16)]
                mm_group(kb, psZ[:], pairs, reads=("wba_b", "wba_a", "xn"), writes=("psZ",))
                kb.op("dve", lambda e, o=zrow[:, ts_ * 512:(ts_ + 1) * 512], s=psZ[:]: e.tensor_copy(o, s),
                      reads=("psZ",), writes=(f"z{which}",))
        zb = zrow_sb[:, 0:HT]
        za = zrow_sb[:, HT:2 * HT]
        kb.op("act", lambda e: e.activation(out=gout_sb[:], in_=zb, func=AF.Sigmoid),
              reads=("z0",), writes=("beta",))
        kb.op("sync", lambda e, t=toff: e.dma_start(out=betarow[:, t:t + HT], in_=gout_sb[:]),
              reads=("beta",), dma=True)
        kb.op("act", lambda e: e.activation(out=t1_sb[:], in_=zb, func=AF.Abs),
              reads=("z0",), writes=("t1",))
        kb.op("act", lambda e: e.activation(out=t1_sb[:], in_=t1_sb[:], func=AF.Exp, scale=-1.0),
              reads=("t1",), writes=("t1",))
        kb.op("act", lambda e: e.activation(out=t3_sb[:], in_=t1_sb[:], func=AF.Ln, bias=one_c[:, 0:1]),
              reads=("t1", one_k), writes=("t3",))
        kb.op("act", lambda e: e.activation(out=t4_sb[:], in_=zb, func=AF.Relu, scale=-1.0),
              reads=("z0",), writes=("t4",))
        kb.op("dve", lambda e: e.scalar_tensor_tensor(out=gout_sb[:], in0=t4_sb[:], scalar=-1.0, in1=t3_sb[:],
                                                      op0=ALU.mult, op1=ALU.subtract),
              reads=("t3", "t4"), writes=("lnb",))
        kb.op("sync", lambda e, t=toff: e.dma_start(out=lnbrow[:, t:t + HT], in_=gout_sb[:]),
              reads=("lnb",), dma=True)
        kb.op("act", lambda e: e.activation(out=t1_sb[:], in_=za, func=AF.Abs, bias=dtbc[:, 0:1]),
              reads=("z1", "dtb"), writes=("t1",))
        kb.op("act", lambda e: e.activation(out=t1_sb[:], in_=t1_sb[:], func=AF.Exp, scale=-1.0),
              reads=("t1",), writes=("t1",))
        kb.op("act", lambda e: e.activation(out=t3_sb[:], in_=t1_sb[:], func=AF.Ln, bias=one_c[:, 0:1]),
              reads=("t1", one_k), writes=("t3",))
        kb.op("act", lambda e: e.activation(out=t4_sb[:], in_=za, func=AF.Relu, bias=dtbc[:, 0:1]),
              reads=("z1", "dtb"), writes=("t4",))
        kb.op("dve", lambda e: e.tensor_tensor(out=t4_sb[:], in0=t4_sb[:], in1=t3_sb[:], op=ALU.add),
              reads=("t3", "t4"), writes=("t4",))
        kb.op("dve", lambda e: e.tensor_scalar(out=gout_sb[:], in0=t4_sb[:], scalar1=nexp_sb[:, 0:1], scalar2=None, op0=ALU.mult),
              reads=("t4", "nexp"), writes=("gv",))
        kb.op("sync", lambda e, t=toff: e.dma_start(out=grow[:, t:t + HT], in_=gout_sb[:]),
              reads=("gv",), dma=True)


def build_l0proj():
    return build_program(emit_l0proj)



def emit_scan(nc, kb, nchunks=T // 128):
    C = 128
    qf = nc.dram_tensor("qf", [2, 128, T], BF16, kind="ExternalInput").ap()
    kf = nc.dram_tensor("kf", [2, 128, T], BF16, kind="ExternalInput").ap()
    ktok = nc.dram_tensor("ktok", [T, 256], BF16, kind="ExternalInput").ap()
    vtok = nc.dram_tensor("vtok", [T, 512], BF16, kind="ExternalInput").ap()
    gcol = nc.dram_tensor("gcol", [T, 8], F32, kind="ExternalInput").ap()
    betacol = nc.dram_tensor("betacol", [T, 8], F32, kind="ExternalInput").ap()
    lnbcol = nc.dram_tensor("lnbcol", [T, 8], F32, kind="ExternalInput").ap()
    lnrkcol = nc.dram_tensor("lnrkcol", [T, 4], F32, kind="ExternalInput").ap()
    lnrqcol = nc.dram_tensor("lnrqcol", [T, 4], F32, kind="ExternalInput").ap()
    uinc = nc.dram_tensor("UINC", [128, 128], F32, kind="ExternalInput").ap()
    i128f = nc.dram_tensor("I128F", [128, 128], F32, kind="ExternalInput").ap()
    i128b = nc.dram_tensor("I128B", [128, 128], BF16, kind="ExternalInput").ap()
    ones1 = nc.dram_tensor("ONES1", [1, 128], F32, kind="ExternalInput").ap()
    oout = nc.dram_tensor("o", [T, 512], BF16, kind="ExternalOutput").ap()

    uinc_sb = nc.alloc_sbuf_tensor("uinc_sb", [128, 128], F32)
    i128f_sb = nc.alloc_sbuf_tensor("i128f_sb", [128, 128], F32)
    i128b_sb = nc.alloc_sbuf_tensor("i128b_sb", [128, 128], BF16)
    ones1_sb = nc.alloc_sbuf_tensor("ones1_sb", [1, 128], F32)
    for nm, dst, src in (("uinc", uinc_sb, uinc), ("i128f", i128f_sb, i128f),
                         ("i128b", i128b_sb, i128b), ("ones1", ones1_sb, ones1)):
        kb.op("sync", lambda e, d=dst, s=src: e.dma_start(out=d[:], in_=s[:, :]), writes=(nm,), dma=True)
    CONSTS = ("uinc", "i128f", "i128b", "ones1")

    S_f = nc.alloc_sbuf_tensor("S_f", [64, 8 * 64], F32)
    S_b = nc.alloc_sbuf_tensor("S_b", [128, 8 * 64], BF16)
    kb.op("dve", lambda e: e.memset(S_f[:], 0.0), writes=("S_f",))
    kb.op("dve", lambda e: e.memset(S_b[:], 0.0), writes=("S_b",))

    NB = 2
    kf_sb = nc.alloc_sbuf_tensor("kf_sb", [128, NB * 2 * 128], BF16)
    qf_sb = nc.alloc_sbuf_tensor("qf_sb", [128, NB * 2 * 128], BF16)
    kt_sb = nc.alloc_sbuf_tensor("kt_sb", [128, NB * 256], BF16)
    vt_sb = nc.alloc_sbuf_tensor("vt_sb", [128, NB * 512], BF16)
    cols_sb = nc.alloc_sbuf_tensor("cols_sb", [128, NB * 32], F32)
    cum_sb = nc.alloc_sbuf_tensor("cum_sb", [128, 8], F32)
    biasA_sb = nc.alloc_sbuf_tensor("biasA_sb", [128, 8], F32)
    rowAc_sb = nc.alloc_sbuf_tensor("rowAc_sb", [128, 8], F32)
    biasP_sb = nc.alloc_sbuf_tensor("biasP_sb", [128, 8], F32)
    rows_sb = nc.alloc_sbuf_tensor("rows_sb", [1, 4 * 128], F32)
    ccol_sb = nc.alloc_sbuf_tensor("ccol_sb", [128, 4], F32)
    arg_sb = nc.alloc_sbuf_tensor("arg_sb", [128, 128], F32)
    dmat_sb = nc.alloc_sbuf_tensor("dmat_sb", [128, 128], F32)
    araw_sb = nc.alloc_sbuf_tensor("araw_sb", [128, 128], F32)
    kq_sb = nc.alloc_sbuf_tensor("kq_sb", [128, 128], F32)
    B_sb = nc.alloc_sbuf_tensor("B_sb", [128, 128], BF16)
    Useq = [nc.alloc_sbuf_tensor(f"U{i}", [128, 128], BF16) for i in range(2)]
    Lseq = [nc.alloc_sbuf_tensor(f"L{i}", [128, 128], BF16) for i in range(2)]
    Pp_sb = nc.alloc_sbuf_tensor("Pp_sb", [128, 128], BF16)
    V_sb = [nc.alloc_sbuf_tensor(f"V{i}", [128, 64], BF16) for i in range(2)]
    R_sb = nc.alloc_sbuf_tensor("R_sb", [128, 64], BF16)
    tmp_sb = nc.alloc_sbuf_tensor("tmp_sb", [128, 64], F32)
    tmp2_sb = nc.alloc_sbuf_tensor("tmp2_sb", [128, 64], BF16)
    kw_sb = nc.alloc_sbuf_tensor("kw_sb", [128, 64], BF16)
    lamc_sb = nc.alloc_sbuf_tensor("lamc_sb", [64, 1], F32)
    o_sb = nc.alloc_sbuf_tensor("o_sb", [128, 512], BF16)

    P = [nc.alloc_psum_tensor(f"P{i}", [128, 512], F32) for i in range(8)]

    for c in range(nchunks):
        t0 = c * C
        bb = c % NB
        kfs = kf_sb[:, bb * 256:(bb + 1) * 256]
        qfs = qf_sb[:, bb * 256:(bb + 1) * 256]
        kts = kt_sb[:, bb * 256:(bb + 1) * 256]
        vts = vt_sb[:, bb * 512:(bb + 1) * 512]
        cls = cols_sb[:, bb * 32:(bb + 1) * 32]
        kL = {"kf": f"kf{bb}", "qf": f"qf{bb}", "kt": f"kt{bb}", "vt": f"vt{bb}", "cl": f"cl{bb}"}
        for jj_ in range(2):
            kb.op("sync", lambda e, o=kfs[:, jj_ * 128:(jj_ + 1) * 128], s=kf[jj_, :, t0:t0 + C]: e.dma_start(out=o, in_=s),
                  writes=(kL["kf"],), dma=True)
            kb.op("sync", lambda e, o=qfs[:, jj_ * 128:(jj_ + 1) * 128], s=qf[jj_, :, t0:t0 + C]: e.dma_start(out=o, in_=s),
                  writes=(kL["qf"],), dma=True)
        kb.op("sync", lambda e, o=kts, s=ktok[t0:t0 + C, :]: e.dma_start(out=o, in_=s),
              writes=(kL["kt"],), dma=True)
        kb.op("sync", lambda e, o=vts, s=vtok[t0:t0 + C, :]: e.dma_start(out=o, in_=s),
              writes=(kL["vt"],), dma=True)
        kb.op("sync", lambda e, o=cls[:, 0:8], s=gcol[t0:t0 + C, :]: e.dma_start(out=o, in_=s),
              writes=(kL["cl"],), dma=True)
        kb.op("sync", lambda e, o=cls[:, 8:16], s=betacol[t0:t0 + C, :]: e.dma_start(out=o, in_=s),
              writes=(kL["cl"] + "b",), dma=True)
        kb.op("sync", lambda e, o=cls[:, 16:24], s=lnbcol[t0:t0 + C, :]: e.dma_start(out=o, in_=s),
              writes=(kL["cl"] + "c",), dma=True)
        kb.op("sync", lambda e, o=cls[:, 24:28], s=lnrkcol[t0:t0 + C, :]: e.dma_start(out=o, in_=s),
              writes=(kL["cl"] + "d",), dma=True)
        kb.op("sync", lambda e, o=cls[:, 28:32], s=lnrqcol[t0:t0 + C, :]: e.dma_start(out=o, in_=s),
              writes=(kL["cl"] + "e",), dma=True)
        CLK = (kL["cl"], kL["cl"] + "b", kL["cl"] + "c", kL["cl"] + "d", kL["cl"] + "e")

        kb.op("pe", lambda e, r=cls[:, 0:8]: e.matmul(P[2][0:128, 0:8], uinc_sb[:], r, start=True, stop=True),
              reads=("uinc",) + CLK, writes=("P2",))
        kb.op("dve", lambda e: e.tensor_copy(cum_sb[:], P[2][0:128, 0:8]), reads=("P2",), writes=("cum",))
        kb.op("dve", lambda e, cl=cls: e.tensor_tensor(out=biasA_sb[:], in0=cum_sb[:], in1=cl[:, 16:24], op=ALU.add),
              reads=("cum",) + CLK, writes=("biasA",))
        kb.op("dve", lambda e, cl=cls: e.tensor_tensor(
            out=biasA_sb[:].rearrange("p (a b) -> p a b", b=2), in0=biasA_sb[:].rearrange("p (a b) -> p a b", b=2),
            in1=cl[:, 24:28].unsqueeze(2).broadcast_to((128, 4, 2)), op=ALU.add),
              reads=("biasA",) + CLK, writes=("biasA",))
        kb.op("dve", lambda e, cl=cls: e.scalar_tensor_tensor(
            out=rowAc_sb[:].rearrange("p (a b) -> p a b", b=2), in0=cum_sb[:].rearrange("p (a b) -> p a b", b=2),
            scalar=-1.0, in1=cl[:, 24:28].unsqueeze(2).broadcast_to((128, 4, 2)), op0=ALU.mult, op1=ALU.add),
              reads=("cum",) + CLK, writes=("rowAc",))
        kb.op("dve", lambda e, cl=cls: e.scalar_tensor_tensor(
            out=biasP_sb[:].rearrange("p (a b) -> p a b", b=2), in0=cum_sb[:].rearrange("p (a b) -> p a b", b=2),
            scalar=1.0, in1=cl[:, 28:32].unsqueeze(2).broadcast_to((128, 4, 2)), op0=ALU.mult, op1=ALU.add),
              reads=("cum",) + CLK, writes=("biasP",))

        for h in range(8):
            khl = h // 2
            jj = khl // 2
            pb = 64 * (khl % 2)
            kft = kfs[pb:pb + 64, jj * 128:(jj + 1) * 128]
            qft = qfs[pb:pb + 64, jj * 128:(jj + 1) * 128]
            HK = (f"h{h}",)
            if h % 2 == 0:
                kb.op("pe", lambda e, l=kft: e.matmul(P[0][:, 0:128], l, l, start=True, stop=True),
                      reads=(kL["kf"],), writes=("P0",))
                kb.op("pe", lambda e, l=kft, r=qft: e.matmul(P[1][:, 0:128], l, r, start=True, stop=True),
                      reads=(kL["kf"], kL["qf"]), writes=("P1",))
                kb.op("dve", lambda e: e.tensor_copy(araw_sb[:], P[0][:, 0:128]), reads=("P0",), writes=("araw",))
                kb.op("dve", lambda e: e.tensor_copy(kq_sb[:], P[1][:, 0:128]), reads=("P1",), writes=("kq",))
            for ri, (colt, colk) in enumerate(((rowAc_sb, "rowAc"), (biasA_sb, "biasA"),
                                               (biasP_sb, "biasP"), (cum_sb, "cum"))):
                kb.op("pe", lambda e, ct=colt, h=h, ri=ri: e.matmul(
                    P[2][0:1, 128 + ri * 128: 128 + (ri + 1) * 128] if False else P[2][0:1, 0:128],
                    ct[:, h:h + 1], i128f_sb[:], start=True, stop=True),
                      reads=(colk, "i128f"), writes=("P2",))
                kb.op("dve", lambda e, ri=ri: e.tensor_copy(rows_sb[:, ri * 128:(ri + 1) * 128], P[2][0:1, 0:128]),
                      reads=("P2",), writes=(f"rows{ri}",))
            ROWK = ("rows0", "rows1", "rows2", "rows3")
            kb.op("pe", lambda e: e.matmul(P[2][0:128, 0:1], ones1_sb[:], rows_sb[:, 3 * 128 + 127: 3 * 128 + 128],
                                           start=True, stop=True),
                  reads=("rows3", "ones1"), writes=("P2",))
            kb.op("dve", lambda e: e.tensor_copy(ccol_sb[:, 0:1], P[2][0:128, 0:1]), reads=("P2",), writes=("cC",))
            kb.op("act", lambda e, h=h: e.activation(out=ccol_sb[:, 1:2], in_=biasA_sb[:, h:h + 1], func=AF.Exp),
                  reads=("biasA",), writes=("rho",))
            kb.op("act", lambda e, h=h: e.activation(out=ccol_sb[:, 2:3], in_=biasP_sb[:, h:h + 1], func=AF.Exp),
                  reads=("biasP",), writes=("rhoP",))
            kb.op("dve", lambda e, h=h: e.tensor_tensor(out=ccol_sb[:, 3:4], in0=rowAc_sb[:, h:h + 1],
                                                        in1=ccol_sb[:, 0:1], op=ALU.add),
                  reads=("rowAc", "cC"), writes=("wc",))
            kb.op("act", lambda e: e.activation(out=ccol_sb[:, 3:4], in_=ccol_sb[:, 3:4], func=AF.Exp),
                  reads=("wc",), writes=("wc",))
            specs = (
                (0, biasA_sb, "biasA", 0, "B"),
                (1, rowAc_sb, "rowAc", 1, "U"),
                (2, rowAc_sb, "rowAc", 2, "Pp"),
            )
            for (rsl, bcol, colk, mk, what) in specs:
                kb.op("pe", lambda e, rsl=rsl: e.matmul(P[2][0:128, 0:128], ones1_sb[:],
                                                        rows_sb[:, rsl * 128:(rsl + 1) * 128], start=True, stop=True),
                      reads=(ROWK[rsl], "ones1"), writes=("P2",))
                kb.op("dve", lambda e, bc=bcol, h=h: e.tensor_scalar(
                    out=arg_sb[:], in0=P[2][0:128, 0:128], scalar1=bc[:, h:h + 1], scalar2=30.0,
                    op0=ALU.add, op1=ALU.min),
                      reads=("P2", colk), writes=("arg",))
                if mk == 0:
                    pat, cm, op = [[-1, 128]], 1, ALU.is_gt
                elif mk == 1:
                    pat, cm, op = [[1, 128]], -1, ALU.is_gt
                else:
                    pat, cm, op = [[1, 128]], -1, ALU.is_ge
                def _mask(e, pat=pat, cm=cm, op=op):
                    if not hasattr(e, "_fillreg"):
                        e._fillreg = e.to_reg(-200.0)
                    return e.affine_select(out=dmat_sb[:], in_=arg_sb[:], pattern=pat, compare_op=op,
                                           fill=e._fillreg, base=0, channel_multiplier=cm)
                kb.op("gp", _mask, reads=("arg",), writes=("dmat",))
                kb.op("act", lambda e: e.activation(out=dmat_sb[:], in_=dmat_sb[:], func=AF.Exp),
                      reads=("dmat",), writes=("dmat",))
                if what == "B":
                    kb.op("dve", lambda e: e.scalar_tensor_tensor(out=B_sb[:], in0=dmat_sb[:], scalar=-1.0,
                                                                  in1=araw_sb[:], op0=ALU.mult, op1=ALU.mult),
                          reads=("dmat", "araw"), writes=("Bm",))
                elif what == "U":
                    kb.op("dve", lambda e: e.scalar_tensor_tensor(out=Useq[0][:], in0=dmat_sb[:], scalar=-1.0,
                                                                  in1=araw_sb[:], op0=ALU.mult, op1=ALU.mult),
                          reads=("dmat", "araw"), writes=("U0",))
                else:
                    kb.op("dve", lambda e: e.scalar_tensor_tensor(out=Pp_sb[:], in0=dmat_sb[:], scalar=1.0,
                                                                  in1=kq_sb[:], op0=ALU.mult, op1=ALU.mult),
                          reads=("dmat", "kq"), writes=("Pp",))
            kb.op("pe", lambda e, l=kft, pb=pb, h=h: e.matmul(P[3][:, 0:64], l, S_b[pb:pb + 64, h * 64:(h + 1) * 64], start=True, stop=True),
                  reads=(kL["kf"], "S_b"), writes=("P3",))
            kb.op("dve", lambda e: e.tensor_scalar(out=tmp_sb[:], in0=P[3][:, 0:64], scalar1=ccol_sb[:, 1:2],
                                                   scalar2=None, op0=ALU.mult),
                  reads=("P3", "rho"), writes=("tmp",))
            kb.op("dve", lambda e, h=h, v=vts, cl=cls: e.scalar_tensor_tensor(
                out=R_sb[:], in0=v[:, h * 64:(h + 1) * 64], scalar=cl[:, 8 + h:9 + h], in1=tmp_sb[:],
                op0=ALU.mult, op1=ALU.subtract),
                  reads=(kL["vt"],) + CLK + ("tmp",), writes=("V0v",))
            Vcur, Vk = R_sb, "V0v"
            Ucur, Uk = Useq[0], "U0"
            Lcur, Lk = B_sb, "Bm"
            for s in range(7):
                Vnext = V_sb[s % 2]
                mm_group(kb, P[3][:, 0:64], [(Ucur[:], Vcur[:]), (i128b_sb[:], Vcur[:])],
                         reads=(Uk, Vk, "i128b"), writes=("P3",))
                kb.op("act", lambda e, o=Vnext: e.activation(out=o[:], in_=P[3][:, 0:64], func=AF.Copy),
                      reads=("P3",), writes=(f"V{s + 1}v",))
                if s < 6:
                    Unext = Useq[(s + 1) % 2]
                    kb.op("pe", lambda e, l=Lcur, r=Ucur: e.matmul(P[4][:, 0:128], l[:], r[:], start=True, stop=True),
                          reads=(Lk, Uk), writes=("P4",))
                    if s < 5:
                        Lnext = Lseq[(s + 1) % 2]
                        kb.op("pe", lambda e, l=Ucur, r=Lcur: e.matmul(P[5][:, 0:128], l[:], r[:], start=True, stop=True),
                              reads=(Uk, Lk), writes=("P5",))
                        kb.op("dve", lambda e, o=Lnext: e.tensor_copy(o[:], P[5][:, 0:128]),
                              reads=("P5",), writes=(f"L{s + 1}m",))
                        Lcur, Lk = Lnext, f"L{s + 1}m"
                    kb.op("act", lambda e, o=Unext: e.activation(out=o[:], in_=P[4][:, 0:128], func=AF.Copy),
                          reads=("P4",), writes=(f"U{s + 1}m",))
                    Ucur, Uk = Unext, f"U{s + 1}m"
                Vcur, Vk = Vnext, f"V{s + 1}v"
            kb.op("pe", lambda e, l=qft, pb=pb, h=h: e.matmul(P[7][:, 0:64], l, S_b[pb:pb + 64, h * 64:(h + 1) * 64], start=True, stop=True),
                  reads=(kL["qf"], "S_b"), writes=("P7",))
            kb.op("dve", lambda e: e.tensor_scalar(out=tmp2_sb[:], in0=P[7][:, 0:64], scalar1=ccol_sb[:, 2:3],
                                                   scalar2=None, op0=ALU.mult),
                  reads=("P7", "rhoP"), writes=("tmp2",))
            mm_group(kb, P[6][:, 0:64], [(Pp_sb[:], Vcur[:]), (i128b_sb[:], tmp2_sb[:])],
                     reads=("Pp", Vk, "tmp2", "i128b"), writes=("P6",))
            kb.op("act", lambda e, h=h: e.activation(out=o_sb[:, h * 64:(h + 1) * 64], in_=P[6][:, 0:64], func=AF.Copy),
                  reads=("P6",), writes=(f"osl{h}",))
            kb.op("dve", lambda e, h=h, kt=kts, khl=khl: e.tensor_scalar(out=kw_sb[:], in0=kt[:, khl * 64:(khl + 1) * 64],
                                                                scalar1=ccol_sb[:, 3:4], scalar2=None, op0=ALU.mult),
                  reads=(kL["kt"], "wc"), writes=("kw",))
            kb.op("pe", lambda e: e.matmul(P[7][0:64, 64:128], kw_sb[:], Vcur[:], start=True, stop=True),
                  reads=("kw", Vk), writes=("P7",))
            kb.op("act", lambda e: e.activation(out=lamc_sb[:], in_=ccol_sb[0:64, 0:1], func=AF.Exp),
                  reads=("cC",), writes=("lamC",))
            kb.op("dve", lambda e, h=h: e.scalar_tensor_tensor(
                out=S_f[:, h * 64:(h + 1) * 64], in0=S_f[:, h * 64:(h + 1) * 64], scalar=lamc_sb[:, 0:1],
                in1=P[7][0:64, 64:128], op0=ALU.mult, op1=ALU.add),
                  reads=("S_f", "lamC", "P7"), writes=("S_f",))
            kb.op("act", lambda e, h=h: e.activation(out=S_b[0:64, h * 64:(h + 1) * 64], in_=S_f[:, h * 64:(h + 1) * 64],
                                                     func=AF.Copy),
                  reads=("S_f",), writes=("S_b",))
            kb.op("act", lambda e, h=h: e.activation(out=S_b[64:128, h * 64:(h + 1) * 64], in_=S_f[:, h * 64:(h + 1) * 64],
                                                     func=AF.Copy),
                  reads=("S_f",), writes=("S_b",))
        kb.op("sync", lambda e, t0=t0: e.dma_start(out=oout[t0:t0 + C, :], in_=o_sb[:]),
              reads=tuple(f"osl{h}" for h in range(8)), dma=True)


def build_scan():
    return build_program(emit_scan)



def emit_epi(nc, kb, NT=2048):
    o_in = nc.dram_tensor("o_in", [NT, 1024], BF16, kind="ExternalInput").ap()
    xnf = nc.dram_tensor("xnf", [D, NT], BF16, kind="ExternalInput").ap()
    wzT = nc.dram_tensor("wzT", [D, 1024], F32, kind="ExternalInput").ap()
    woutT = nc.dram_tensor("woutT", [1024, D], F32, kind="ExternalInput").ap()
    nw128 = nc.dram_tensor("nw128", [128, 1], F32, kind="ExternalInput").ap()
    attnf = nc.dram_tensor("attnf", [D, NT], BF16, kind="ExternalOutput").ap()
    yn_scr = nc.dram_tensor("yn_scr", [NT, 1024], BF16, kind="Internal").ap()

    xn_sb = nc.alloc_sbuf_tensor("xn_sb", [128, 16 * NT], BF16)
    wz_sb = nc.alloc_sbuf_tensor("wz_sb", [128, 16 * 1024], BF16)
    wo_sb = nc.alloc_sbuf_tensor("wo_sb", [128, 8 * 2048], BF16)
    ynf_sb = nc.alloc_sbuf_tensor("ynf_sb", [128, 8 * 2048], BF16)
    nw_sb = nc.alloc_sbuf_tensor("nw_sb", [128, 1], F32)
    zs_sb = nc.alloc_sbuf_tensor("zs_sb", [128, 2 * 1024], BF16)
    ot_sb = nc.alloc_sbuf_tensor("ot_sb", [128, 2 * 1024], BF16)
    ssq_sb = nc.alloc_sbuf_tensor("ssq_sb", [128, 16], F32)
    rstd_sb = nc.alloc_sbuf_tensor("rstd_sb", [128, 16], F32)
    ynt_sb = nc.alloc_sbuf_tensor("ynt_sb", [128, 2 * 1024], BF16)
    osq_sb = nc.alloc_sbuf_tensor("osq_sb", [128, 1024], BF16)
    out_sb = nc.alloc_sbuf_tensor("out_sb", [128, 512], BF16)
    psA = nc.alloc_psum_tensor("psA", [128, 512], F32)
    psB = nc.alloc_psum_tensor("psB", [128, 512], F32)
    eps_c, eps_k = kb.constcol(nc, 1e-6)

    kb.op("sync", lambda e: e.dma_start(out=xn_sb[:].rearrange("p (k t) -> p k t", k=16),
                                        in_=xnf.rearrange("(k p) t -> p k t", p=128)),
          writes=("xn",), dma=True)
    kb.op("gp", lambda e: e.dma_start(out=wz_sb[:].rearrange("p (k m) -> p k m", k=16),
                                      in_=wzT.rearrange("(k p) m -> p k m", p=128)),
          writes=("wz",), dma=True)
    kb.op("gp", lambda e: e.dma_start(out=wo_sb[:].rearrange("p (k m) -> p k m", k=8),
                                      in_=woutT.rearrange("(k p) m -> p k m", p=128)),
          writes=("wo",), dma=True)
    kb.op("sync", lambda e: e.dma_start(out=nw_sb[:], in_=nw128[:, :]), writes=("nw",), dma=True)
    for ks in range(8):
        kb.op("dve", lambda e, ks=ks: e.tensor_scalar(out=wo_sb[:, ks * 2048:(ks + 1) * 2048],
                                                      in0=wo_sb[:, ks * 2048:(ks + 1) * 2048],
                                                      scalar1=nw_sb[:, 0:1], scalar2=None, op0=ALU.mult),
              reads=("wo", "nw"), writes=("wo",))

    nt = NT // 128
    for i in range(nt):
        pp = i % 2
        zt = zs_sb[:, pp * 1024:(pp + 1) * 1024]
        ot = ot_sb[:, pp * 1024:(pp + 1) * 1024]
        ynt = ynt_sb[:, pp * 1024:(pp + 1) * 1024]
        ko, kz, ky = f"ot{pp}", f"zs{pp}", f"ynt{pp}"
        kb.op("sync", lambda e, i=i, o=ot: e.dma_start(out=o, in_=o_in[i * 128:(i + 1) * 128, :]),
              writes=(ko,), dma=True)
        for zs_ in range(2):
            ps = psA if zs_ == 0 else psB
            psk = "psA" if zs_ == 0 else "psB"
            pairs = [(xn_sb[:, k * NT + i * 128: k * NT + (i + 1) * 128],
                      wz_sb[:, k * 1024 + zs_ * 512: k * 1024 + (zs_ + 1) * 512]) for k in range(16)]
            mm_group(kb, ps[:], pairs, reads=("xn", "wz"), writes=(psk,))
            kb.op("act", lambda e, o=zt[:, zs_ * 512:(zs_ + 1) * 512], s=ps[:]: e.activation(out=o, in_=s, func=AF.Silu),
                  reads=(psk,), writes=(kz,))
        kb.op("dve", lambda e, o=ot: e.scalar_tensor_tensor(out=osq_sb[:], in0=o, scalar=1.0, in1=o,
                                                            op0=ALU.mult, op1=ALU.mult),
              reads=(ko,), writes=("osq",))
        kb.op("dve", lambda e: e.tensor_reduce(out=ssq_sb[:], in_=osq_sb[:].rearrange("p (h d) -> p h d", d=64),
                                               axis=AX.X, op=ALU.add),
              reads=("osq",), writes=("ssq",))
        kb.op("act", lambda e: e.activation(out=rstd_sb[:], in_=ssq_sb[:], func=AF.Ln, bias=eps_c[:, 0:1], scale=1.0 / 64),
              reads=("ssq", eps_k), writes=("rstd",))
        kb.op("act", lambda e: e.activation(out=rstd_sb[:], in_=rstd_sb[:], func=AF.Exp, scale=-0.5),
              reads=("rstd",), writes=("rstd",))
        kb.op("dve", lambda e, o=ot, y=ynt: e.tensor_tensor(
            out=y.rearrange("p (h d) -> p h d", d=64), in0=o.rearrange("p (h d) -> p h d", d=64),
            in1=rstd_sb[:].unsqueeze(2).broadcast_to((128, 16, 64)), op=ALU.mult),
              reads=(ko, "rstd"), writes=(ky,))
        kb.op("dve", lambda e, y=ynt, z=zt: e.tensor_tensor(out=y, in0=y, in1=z, op=ALU.mult),
              reads=(ky, kz), writes=(ky,))
        kb.op("sync", lambda e, i=i, y=ynt: e.dma_start(out=yn_scr[i * 128:(i + 1) * 128, :], in_=y),
              reads=(ky,), writes=(f"ysc{i}",), dma=True)
    for c in range(8):
        kb.op("sync", lambda e, c=c: e.dma_start_transpose(out=ynf_sb[:, c * 2048:(c + 1) * 2048],
                                                           in_=yn_scr[:, c * 128:(c + 1) * 128]),
              reads=tuple(f"ysc{i}" for i in range(16)), writes=(f"ynf{c}",), dma=True)
    for m in range(16):
        for ts_ in range(NT // 512):
            ps = psA if ts_ % 2 == 0 else psB
            psk = "psA" if ts_ % 2 == 0 else "psB"
            pairs = [(wo_sb[:, k * 2048 + m * 128:(k * 2048 + (m + 1) * 128)],
                      ynf_sb[:, k * 2048 + ts_ * 512: k * 2048 + (ts_ + 1) * 512]) for k in range(8)]
            mm_group(kb, ps[:], pairs, reads=("wo",) + tuple(f"ynf{c}" for c in range(8)), writes=(psk,))
            kb.op("act", lambda e, s=ps[:]: e.activation(out=out_sb[:], in_=s, func=AF.Copy),
                  reads=(psk,), writes=("outp",))
            kb.op("sync", lambda e, m=m, ts_=ts_: e.dma_start(out=attnf[m * 128:(m + 1) * 128, ts_ * 512:(ts_ + 1) * 512],
                                                              in_=out_sb[:]),
                  reads=("outp",), dma=True)


def build_epi():
    return build_program(emit_epi)



def emit_ffn(nc, kb, NT=2048):
    xnf = nc.dram_tensor("xnf", [D, NT], BF16, kind="ExternalInput").ap()
    h2048 = nc.dram_tensor("H2048", [D, D], BF16, kind="ExternalInput").ap()
    h128 = nc.dram_tensor("H128", [128, 128], BF16, kind="ExternalInput").ap()
    suug = nc.dram_tensor("suug", [128, 16], F32, kind="ExternalInput").ap()
    sudp = nc.dram_tensor("sudp", [128, 64], F32, kind="ExternalInput").ap()
    wsug = nc.dram_tensor("wsug", [128, 128], F32, kind="ExternalInput").ap()
    wsdp = nc.dram_tensor("wsdp", [128, 16], F32, kind="ExternalInput").ap()
    wupT = nc.dram_tensor("wupT", [D, 8192], F32, kind="ExternalInput").ap()
    wgT = nc.dram_tensor("wgT", [D, 8192], F32, kind="ExternalInput").ap()
    d2T = nc.dram_tensor("d2T", [64, 128, D], BF16, kind="ExternalInput").ap()
    ffnf = nc.dram_tensor("ffnf", [D, NT], BF16, kind="ExternalOutput").ap()
    y_scr = nc.dram_tensor("y_scr", [8192, NT], BF16, kind="Internal").ap()

    hu_sb = nc.alloc_sbuf_tensor("hu_sb", [128, 16 * NT], BF16)
    hh_sb = nc.alloc_sbuf_tensor("hh_sb", [128, 16 * 2048], BF16)
    xns_sb = nc.alloc_sbuf_tensor("xns_sb", [128, 16 * 512], BF16)
    w_sb = nc.alloc_sbuf_tensor("w_sb", [128, 2 * 8192], BF16)
    h128_sb = nc.alloc_sbuf_tensor("h128_sb", [128, 128], BF16)
    sc_sb = nc.alloc_sbuf_tensor("sc_sb", [128, 16 + 64 + 64 + 64 + 16 + 16], F32)
    sg_sb = nc.alloc_sbuf_tensor("sg_sb", [128, 512], F32)
    yt_sb = nc.alloc_sbuf_tensor("yt_sb", [128, 2 * 512], BF16)
    out_sb = nc.alloc_sbuf_tensor("out_sb", [128, 2 * 512], BF16)
    psA = nc.alloc_psum_tensor("psA", [128, 512], F32)
    psB = nc.alloc_psum_tensor("psB", [128, 512], F32)
    psC = nc.alloc_psum_tensor("psC", [128, 512], F32)
    psD = nc.alloc_psum_tensor("psD", [128, 512], F32)

    SQH = float(2.0 ** -0.5)
    kb.op("sync", lambda e: e.dma_start(out=hh_sb[:].rearrange("p (k m) -> p k m", k=16),
                                        in_=h2048.rearrange("(k p) m -> p k m", p=128)),
          writes=tuple(f"H{k}" for k in range(16)), dma=True)
    kb.op("sync", lambda e: e.dma_start(out=h128_sb[:], in_=h128[:, :]), writes=("h128",), dma=True)
    kb.op("sync", lambda e: e.dma_start(out=sc_sb[:, 0:16], in_=suug[:, :]), writes=("suug",), dma=True)
    kb.op("sync", lambda e: e.dma_start(out=sc_sb[:, 16:80], in_=sudp[:, :]), writes=("sudp",), dma=True)
    kb.op("sync", lambda e: e.dma_start(out=sc_sb[:, 80:144], in_=wsug[:, 0:64]), writes=("wsu",), dma=True)
    kb.op("sync", lambda e: e.dma_start(out=sc_sb[:, 144:208], in_=wsug[:, 64:128]), writes=("wsg",), dma=True)
    kb.op("sync", lambda e: e.dma_start(out=sc_sb[:, 208:224], in_=wsdp[:, :]), writes=("wsd",), dma=True)
    kb.op("dve", lambda e: e.tensor_tensor(out=sc_sb[:, 80:144], in0=sc_sb[:, 80:144], in1=sc_sb[:, 16:80], op=ALU.mult),
          reads=("wsu", "sudp"), writes=("wsu",))
    kb.op("dve", lambda e: e.tensor_scalar(out=sc_sb[:, 80:144], in0=sc_sb[:, 80:144], scalar1=SQH, scalar2=None, op0=ALU.mult),
          reads=("wsu",), writes=("wsu",))
    kb.op("dve", lambda e: e.tensor_scalar(out=sc_sb[:, 144:208], in0=sc_sb[:, 144:208], scalar1=SQH, scalar2=None, op0=ALU.mult),
          reads=("wsg",), writes=("wsg",))
    kb.op("dve", lambda e: e.tensor_scalar(out=sc_sb[:, 208:224], in0=sc_sb[:, 208:224], scalar1=SQH, scalar2=None, op0=ALU.mult),
          reads=("wsd",), writes=("wsd",))

    for ts_ in range(NT // 512):
        for k in range(16):
            kxn = f"xns{k}"
            kb.op("sync", lambda e, k=k, ts_=ts_: e.dma_start(
                out=xns_sb[:, k * 512:(k + 1) * 512],
                in_=xnf[k * 128:(k + 1) * 128, ts_ * 512:(ts_ + 1) * 512]),
                  writes=(kxn,), dma=True)
            kb.op("dve", lambda e, k=k: e.tensor_scalar(out=xns_sb[:, k * 512:(k + 1) * 512],
                                                        in0=xns_sb[:, k * 512:(k + 1) * 512],
                                                        scalar1=sc_sb[:, k:k + 1], scalar2=None, op0=ALU.mult),
                  reads=(kxn, "suug"), writes=(kxn,))
        for m in range(16):
            ps = psA if m % 2 == 0 else psB
            psk = "psA" if m % 2 == 0 else "psB"
            pairs = [(hh_sb[:, k * 2048 + m * 128:(k * 2048 + (m + 1) * 128)],
                      xns_sb[:, k * 512:(k + 1) * 512]) for k in range(16)]
            mm_group(kb, ps[:], pairs, reads=tuple(f"xns{k}" for k in range(16)) + tuple(f"H{k}" for k in range(16)),
                     writes=(psk,))
            kb.op("act", lambda e, m=m, ts_=ts_, s=ps[:]: e.activation(
                out=hu_sb[:, m * NT + ts_ * 512: m * NT + (ts_ + 1) * 512], in_=s, func=AF.Copy),
                  reads=(psk,), writes=(f"h{m}",))
    HK = tuple(f"h{m}" for m in range(16))
    for io in range(64):
        wpp = io % 2
        wu = w_sb[:, wpp * 2 * 2048: wpp * 2 * 2048 + 2048]
        wg = w_sb[:, wpp * 2 * 2048 + 2048: (wpp + 1) * 2 * 2048]
        kwu, kwg = f"wu{wpp}", f"wg{wpp}"
        kb.op("gp", lambda e, io=io, o=wu: e.dma_start(
            out=o.rearrange("p (k m) -> p k m", k=16),
            in_=wupT.rearrange("(k p) m -> p k m", p=128)[:, :, io * 128:(io + 1) * 128]),
              writes=(kwu,), dma=True)
        kb.op("gp", lambda e, io=io, o=wg: e.dma_start(
            out=o.rearrange("p (k m) -> p k m", k=16),
            in_=wgT.rearrange("(k p) m -> p k m", p=128)[:, :, io * 128:(io + 1) * 128]),
              writes=(kwg,), dma=True)
        for ts_ in range(NT // 512):
            ypp = ts_ % 2
            yt = yt_sb[:, ypp * 512:(ypp + 1) * 512]
            kyt = f"yt{ypp}"
            pairs_u = [(wu[:, k * 128:(k + 1) * 128], hu_sb[:, k * NT + ts_ * 512: k * NT + (ts_ + 1) * 512])
                       for k in range(16)]
            pairs_g = [(wg[:, k * 128:(k + 1) * 128], hu_sb[:, k * NT + ts_ * 512: k * NT + (ts_ + 1) * 512])
                       for k in range(16)]
            mm_group(kb, psC[:], pairs_u, reads=(kwu,) + HK, writes=("psC",))
            mm_group(kb, psD[:], pairs_g, reads=(kwg,) + HK, writes=("psD",))
            kb.op("act", lambda e, io=io: e.activation(out=sg_sb[:], in_=psD[:], func=AF.Silu,
                                                       scale=sc_sb[:, 144 + io:145 + io]),
                  reads=("psD", "wsg"), writes=("sg",))
            kb.op("dve", lambda e, io=io, y=yt: e.scalar_tensor_tensor(
                out=y, in0=sg_sb[:], scalar=sc_sb[:, 80 + io:81 + io],
                in1=psC[:], op0=ALU.mult, op1=ALU.mult),
                  reads=("sg", "wsu", "psC"), writes=(kyt,))
            kb.op("sync", lambda e, io=io, ts_=ts_, y=yt: e.dma_start(
                out=y_scr[io * 128:(io + 1) * 128, ts_ * 512:(ts_ + 1) * 512], in_=y),
                  reads=(kyt,), writes=(f"ysc{io}_{ts_}",), dma=True)
    for ts_ in range(NT // 512):
        for c in range(64):
            ku = f"u{c}"
            khs = f"h{c // 4}"
            kb.op("sync", lambda e, c=c, ts_=ts_: e.dma_start(
                out=xns_sb[:, (c % 16) * 512:(c % 16 + 1) * 512],
                in_=y_scr[c * 128:(c + 1) * 128, ts_ * 512:(ts_ + 1) * 512]),
                  reads=(f"ysc{c}_{ts_}",), writes=(f"yl{c % 16}", f"xns{c % 16}"), dma=True)
            ps = psA if c % 2 == 0 else psB
            psk = "psA" if c % 2 == 0 else "psB"
            kb.op("pe", lambda e, c=c, s=ps: e.matmul(s[:], h128_sb[:], xns_sb[:, (c % 16) * 512:(c % 16 + 1) * 512],
                                                      start=True, stop=True),
                  reads=(f"yl{c % 16}", "h128"), writes=(psk,))
            eng = "act" if c % 2 == 0 else "dve"
            if eng == "act":
                kb.op("act", lambda e, c=c, s=ps: e.activation(out=hu_sb[:, c * 512:(c + 1) * 512], in_=s[:], func=AF.Copy),
                      reads=(psk,), writes=(ku, khs))
            else:
                kb.op("dve", lambda e, c=c, s=ps: e.tensor_copy(hu_sb[:, c * 512:(c + 1) * 512], s[:]),
                      reads=(psk,), writes=(ku, khs))
        UK = tuple(f"u{c}" for c in range(64))
        for o in range(16):
            wpp = o % 2
            dsl = w_sb[:, wpp * 8192:(wpp + 1) * 8192]
            kds = f"ds{wpp}"
            extraw = ("wu0", "wg0", "wu1", "wg1") if wpp == 0 else ()
            kb.op("gp", lambda e, o=o, dd=dsl: e.dma_start(
                out=dd.rearrange("p (c m) -> p c m", c=64),
                in_=d2T[:, :, o * 128:(o + 1) * 128].rearrange("c p m -> p c m")),
                  writes=(kds,) + extraw, dma=True)
            ps = psC if o % 2 == 0 else psD
            psk = "psC" if o % 2 == 0 else "psD"
            pairs = [(dsl[:, c * 128:(c + 1) * 128], hu_sb[:, c * 512:(c + 1) * 512]) for c in range(64)]
            mm_group(kb, ps[:], pairs, reads=(kds,) + UK, writes=(psk,))
            op2 = out_sb[:, (o % 2) * 512:(o % 2 + 1) * 512]
            kb.op("dve", lambda e, o=o, s=ps, oo=op2: e.tensor_scalar(
                out=oo, in0=s[:], scalar1=sc_sb[:, 208 + o:209 + o], scalar2=None, op0=ALU.mult),
                  reads=(psk, "wsd"), writes=(f"ob{o % 2}",))
            kb.op("sync", lambda e, o=o, ts_=ts_, oo=op2: e.dma_start(
                out=ffnf[o * 128:(o + 1) * 128, ts_ * 512:(ts_ + 1) * 512], in_=oo),
                  reads=(f"ob{o % 2}",), dma=True)


def build_ffn():
    return build_program(emit_ffn)



def emit_proj(nc, kb, projs, NT=2048):
    xnf = nc.dram_tensor("xnf", [D, NT], BF16, kind="ExternalInput").ap()
    xn_sb = nc.alloc_sbuf_tensor("xn_sb", [128, 16 * NT], BF16)
    w_sb = nc.alloc_sbuf_tensor("w_sb", [128, 16 * 2048], BF16)
    b_sb = nc.alloc_sbuf_tensor("b_sb", [128, 16], F32)
    out_sb = nc.alloc_sbuf_tensor("out_sb", [128, 2 * 512], BF16)
    psA = nc.alloc_psum_tensor("psA", [128, 512], F32)
    psB = nc.alloc_psum_tensor("psB", [128, 512], F32)
    kb.op("sync", lambda e: e.dma_start(out=xn_sb[:].rearrange("p (k t) -> p k t", k=16),
                                        in_=xnf.rearrange("(k p) t -> p k t", p=128)),
          writes=("xn",), dma=True)
    zero_c, zero_k = kb.constcol(nc, 0.0)
    for (wname, bname, scale, outname) in projs:
        wT = nc.dram_tensor(wname, [D, D], F32, kind="ExternalInput").ap()
        ov = nc.dram_tensor(outname, [D, NT], BF16, kind="ExternalOutput").ap()
        if bname:
            bv = nc.dram_tensor(bname, [128, 16], F32, kind="ExternalInput").ap()
            kb.op("sync", lambda e, bv=bv: e.dma_start(out=b_sb[:], in_=bv[:, :]), writes=("b",), dma=True)
            if scale != 1.0:
                kb.op("dve", lambda e, sc=scale: e.tensor_scalar(out=b_sb[:], in0=b_sb[:], scalar1=sc,
                                                                 scalar2=None, op0=ALU.mult),
                      reads=("b",), writes=("b",))
        kb.op("gp", lambda e, wT=wT: e.dma_start(out=w_sb[:].rearrange("p (k m) -> p k m", k=16),
                                                 in_=wT.rearrange("(k p) m -> p k m", p=128)),
              writes=("w",), dma=True)
        for m in range(16):
            for ts_ in range(NT // 512):
                ps = psA if ts_ % 2 == 0 else psB
                psk = "psA" if ts_ % 2 == 0 else "psB"
                pairs = [(w_sb[:, k * 2048 + m * 128:(k * 2048 + (m + 1) * 128)],
                          xn_sb[:, k * NT + ts_ * 512: k * NT + (ts_ + 1) * 512]) for k in range(16)]
                mm_group(kb, ps[:], pairs, reads=("w", "xn"), writes=(psk,))
                ob = out_sb[:, (ts_ % 2) * 512:(ts_ % 2 + 1) * 512]
                bias_ap = b_sb[:, m:m + 1] if bname else zero_c[:, 0:1]
                kb.op("dve", lambda e, s=ps[:], ob=ob, ba=bias_ap, sc=scale: e.tensor_scalar(
                    out=ob, in0=s, scalar1=sc, scalar2=ba, op0=ALU.mult, op1=ALU.add),
                      reads=(psk, "b" if bname else zero_k), writes=(f"ob{ts_ % 2}",))
                kb.op("sync", lambda e, m=m, ts_=ts_, ob=ob, ov=ov: e.dma_start(
                    out=ov[m * 128:(m + 1) * 128, ts_ * 512:(ts_ + 1) * 512], in_=ob),
                      reads=(f"ob{ts_ % 2}",), dma=True)


def build_qkv1():
    return build_program(lambda nc, kb: emit_proj(nc, kb, [
        ("wqT", "bq", float(128 ** -0.5), "Qf"), ("wkT", "bk", 1.0, "Kf"), ("wvT", "bv", 1.0, "Vf")]))


def build_wo():
    return build_program(lambda nc, kb: emit_proj(nc, kb, [("woT", "bo", 1.0, "attnf")]))



def emit_attn(nc, kb):
    Qf = nc.dram_tensor("Qf", [1024, T], BF16, kind="ExternalInput").ap()
    Kf = nc.dram_tensor("Kf", [1024, T], BF16, kind="ExternalInput").ap()
    Vtok = nc.dram_tensor("Vtok", [T, 1024], BF16, kind="ExternalInput").ap()
    ones1b = nc.dram_tensor("ONES1B", [1, 128], BF16, kind="ExternalInput").ap()
    onescb = nc.dram_tensor("ONESCB", [128, 1], BF16, kind="ExternalInput").ap()
    ctxf = nc.dram_tensor("ctxf", [1024, T], BF16, kind="ExternalOutput").ap()

    qh_sb = nc.alloc_sbuf_tensor("qh_sb", [128, T], BF16)
    kh_sb = nc.alloc_sbuf_tensor("kh_sb", [128, T], BF16)
    q2_sb = nc.alloc_sbuf_tensor("q2_sb", [65, T], BF16)
    k2_sb = nc.alloc_sbuf_tensor("k2_sb", [65, T], BF16)
    vh_sb = nc.alloc_sbuf_tensor("vh_sb", [128, 32 * 128], BF16)
    sqf_sb = nc.alloc_sbuf_tensor("sqf_sb", [128, T], BF16)
    row_sb = nc.alloc_sbuf_tensor("row_sb", [1, T], F32)
    mrow_sb = nc.alloc_sbuf_tensor("mrow_sb", [1, T], F32)
    kr_sb = nc.alloc_sbuf_tensor("kr_sb", [1, T], F32)
    sc1_sb = nc.alloc_sbuf_tensor("sc1_sb", [1, 4], F32)
    pt_sb = nc.alloc_sbuf_tensor("pt_sb", [128, 2 * 1024], BF16)
    invl_sb = nc.alloc_sbuf_tensor("invl_sb", [1, 1024], F32)
    invb_sb = nc.alloc_sbuf_tensor("invb_sb", [128, 1024], BF16)
    ctx_sb = nc.alloc_sbuf_tensor("ctx_sb", [128, 1024], BF16)
    o1b_sb = nc.alloc_sbuf_tensor("o1b_sb", [1, 128], BF16)
    ocb_sb = nc.alloc_sbuf_tensor("ocb_sb", [128, 1], BF16)
    psS = nc.alloc_psum_tensor("psS", [128, 1024], F32)
    psCx = nc.alloc_psum_tensor("psCx", [128, 1024], F32)
    psL = nc.alloc_psum_tensor("psL", [1, 1024], F32)
    psR = nc.alloc_psum_tensor("psR", [1, 512], F32)
    psBc = psS

    kb.op("sync", lambda e: e.dma_start(out=o1b_sb[:], in_=ones1b[:, :]), writes=("o1b",), dma=True)
    o1f_sb = nc.alloc_sbuf_tensor("o1f_sb", [1, 128], F32)
    kb.op("dve", lambda e: e.memset(o1f_sb[:], 1.0), writes=("o1f",))
    kb.op("sync", lambda e: e.dma_start(out=ocb_sb[:], in_=onescb[:, :]), writes=("ocb",), dma=True)

    for h in range(8):
        r0 = h * 128
        kb.op("sync", lambda e, r0=r0: e.dma_start(out=qh_sb[:], in_=Qf[r0:r0 + 128, :]), writes=("qh",), dma=True)
        kb.op("sync", lambda e, r0=r0: e.dma_start(out=kh_sb[:], in_=Kf[r0:r0 + 128, :]), writes=("kh",), dma=True)
        kb.op("sync", lambda e, r0=r0: e.dma_start(
            out=vh_sb[:].rearrange("p (n d) -> p n d", n=32),
            in_=Vtok.rearrange("(n p) c -> p n c", p=128)[:, :, r0:r0 + 128]),
              writes=("vh",), dma=True)
        kb.op("dve", lambda e: e.tensor_copy(k2_sb[0:64, :], kh_sb[64:128, :]), reads=("kh",), writes=("k2",))
        kb.op("dve", lambda e: e.memset(k2_sb[64:65, :], 1.0), writes=("k2o",))
        kb.op("dve", lambda e: e.tensor_copy(q2_sb[0:64, :], qh_sb[64:128, :]), reads=("qh",), writes=("q2",))
        kb.op("act", lambda e: e.activation(out=sqf_sb[:], in_=qh_sb[:], func=AF.Square), reads=("qh",), writes=("sqf",))
        for ts_ in range(T // 512):
            kb.op("pe", lambda e, ts_=ts_: e.matmul(psR[0:1, 0:512], ocb_sb[:], sqf_sb[:, ts_ * 512:(ts_ + 1) * 512],
                                                    start=True, stop=True),
                  reads=("sqf", "ocb"), writes=("psR",))
            kb.op("dve", lambda e, ts_=ts_: e.tensor_copy(row_sb[:, ts_ * 512:(ts_ + 1) * 512], psR[0:1, 0:512]),
                  reads=("psR",), writes=("qsqr",))
        kb.op("act", lambda e: e.activation(out=sqf_sb[:], in_=kh_sb[:], func=AF.Square), reads=("kh",), writes=("sqf",))
        for ts_ in range(T // 512):
            kb.op("pe", lambda e, ts_=ts_: e.matmul(psR[0:1, 0:512], ocb_sb[:], sqf_sb[:, ts_ * 512:(ts_ + 1) * 512],
                                                    start=True, stop=True),
                  reads=("sqf", "ocb"), writes=("psR",))
            kb.op("dve", lambda e, ts_=ts_: e.tensor_copy(kr_sb[:, ts_ * 512:(ts_ + 1) * 512], psR[0:1, 0:512]),
                  reads=("psR",), writes=("ksqr",))
        kb.op("dve", lambda e: e.tensor_reduce(out=sc1_sb[:, 0:1], in_=kr_sb[:], axis=AX.X, op=ALU.max),
              reads=("ksqr",), writes=("mk2",))
        kb.op("act", lambda e: e.activation(out=sc1_sb[:, 1:2], in_=sc1_sb[:, 0:1], func=AF.Ln),
              reads=("mk2",), writes=("lnmk",))
        kb.op("dve", lambda e: e.tensor_scalar(out=sc1_sb[:, 1:2], in0=sc1_sb[:, 1:2], scalar1=0.5, scalar2=0.1,
                                               op0=ALU.mult, op1=ALU.add),
              reads=("lnmk",), writes=("lnmk",))
        kb.op("act", lambda e: e.activation(out=mrow_sb[:], in_=row_sb[:], func=AF.Ln), reads=("qsqr",), writes=("mrow",))
        kb.op("act", lambda e: e.activation(out=mrow_sb[:], in_=mrow_sb[:], func=AF.Exp, scale=0.5, bias=sc1_sb[:, 1:2]),
              reads=("mrow", "lnmk"), writes=("mrow",))
        kb.op("dve", lambda e: e.tensor_scalar(out=q2_sb[64:65, :], in0=mrow_sb[:], scalar1=-1.0, scalar2=None, op0=ALU.mult),
              reads=("mrow",), writes=("q2o",))
        for sp in range(4):
            q0 = sp * 1024
            nk = (sp + 1) * 8
            for kt in range(nk):
                ppp = kt % 2
                pt = pt_sb[:, ppp * 1024:(ppp + 1) * 1024]
                kpt = f"pt{ppp}"
                for sub in range(2):
                    qs = q0 + sub * 512
                    pairs = [(kh_sb[0:64, kt * 128:(kt + 1) * 128], qh_sb[0:64, qs:qs + 512]),
                             (k2_sb[0:65, kt * 128:(kt + 1) * 128], q2_sb[0:65, qs:qs + 512])]
                    mm_group(kb, psS[:, sub * 512:(sub + 1) * 512], pairs,
                             reads=("kh", "qh", "k2", "k2o", "q2", "q2o"), writes=(f"psS{sub}",))
                kb.op("act", lambda e, pt=pt: e.activation(out=pt, in_=psS[:], func=AF.Exp),
                      reads=("psS0", "psS1"), writes=(kpt,))
                if kt * 128 >= q0:
                    def _mask2(e, pt=pt, base=q0 - kt * 128):
                        if not hasattr(e, "_fillreg0"):
                            e._fillreg0 = e.to_reg(0.0)
                        return e.affine_select(out=pt, in_=pt, pattern=[[1, 1024]], compare_op=ALU.is_ge,
                                               fill=e._fillreg0, base=base, channel_multiplier=-1)
                    kb.op("gp", _mask2, reads=(kpt,), writes=(kpt,))
                for sub in range(2):
                    kb.op("pe", lambda e, kt=kt, sub=sub, pt=pt, nk=nk: e.matmul(
                        psCx[:, sub * 512:(sub + 1) * 512], vh_sb[:, kt * 128:(kt + 1) * 128],
                        pt[:, sub * 512:(sub + 1) * 512], start=(kt == 0), stop=(kt == nk - 1)),
                          reads=("vh", kpt), writes=(f"psCx{sub}",))
                    kb.op("pe", lambda e, kt=kt, sub=sub, pt=pt, nk=nk: e.matmul(
                        psL[0:1, sub * 512:(sub + 1) * 512], ocb_sb[:],
                        pt[:, sub * 512:(sub + 1) * 512], start=(kt == 0), stop=(kt == nk - 1)),
                          reads=("ocb", kpt), writes=(f"psL{sub}",))
            kb.op("act", lambda e: e.activation(out=invl_sb[:], in_=psL[0:1, 0:1024], func=AF.Ln),
                  reads=("psL0", "psL1"), writes=("invl",))
            kb.op("act", lambda e: e.activation(out=invl_sb[:], in_=invl_sb[:], func=AF.Exp, scale=-1.0),
                  reads=("invl",), writes=("invl",))
            for sub in range(2):
                kb.op("pe", lambda e, sub=sub: e.matmul(psBc[:, sub * 512:(sub + 1) * 512], o1f_sb[:],
                                                        invl_sb[0:1, sub * 512:(sub + 1) * 512], start=True, stop=True),
                      reads=("invl", "o1f"), writes=(f"psS{sub}",))
            kb.op("dve", lambda e: e.tensor_copy(invb_sb[:], psBc[:]), reads=("psS0", "psS1"), writes=("invb",))
            kb.op("dve", lambda e: e.scalar_tensor_tensor(out=ctx_sb[:], in0=invb_sb[:], scalar=1.0, in1=psCx[:],
                                                          op0=ALU.mult, op1=ALU.mult),
                  reads=("invb", "psCx0", "psCx1"), writes=("ctx",))
            kb.op("sync", lambda e, r0=r0, q0=q0: e.dma_start(out=ctxf[r0:r0 + 128, q0:q0 + 1024], in_=ctx_sb[:]),
                  reads=("ctx",), dma=True)


def build_attn():
    return build_program(emit_attn)



def _split_tokens(arr, n=NCORES):
    flat = np.ascontiguousarray(arr.reshape(-1, arr.shape[-1]))
    return np.split(flat, n, axis=0)


def kernel(**inputs):
    NT = (B * T) // NCORES
    CORES = list(range(NCORES))
    f32 = np.float32
    xfull = np.ascontiguousarray(inputs["L0_attn_x"].astype(f32, copy=False)).reshape(-1, D)

    H64 = hadamard(64)
    H2048c = (hadamard(2048) * (2.0 ** -5)).astype(BF)
    H128c = (hadamard(128) * (2.0 ** -6)).astype(BF)
    UINC = np.triu(np.ones((128, 128), f32))
    I128F = np.eye(128, dtype=f32)
    I128B = I128F.astype(BF)
    ONES1 = np.ones((1, 128), f32)
    ONES1B = ONES1.astype(BF)
    ONESCB = np.ones((128, 1), f32).astype(BF)
    ones2 = np.zeros((128, 2), f32)
    ones2[0:64, 0] = 1.0
    ones2[64:128, 1] = 1.0
    ones2 = ones2.astype(BF)

    def ct(a):
        return np.ascontiguousarray(a)

    nc_fold = build_fold()
    OS = D // NCORES
    wd0 = inputs["L0_ffn_W_down"].astype(f32, copy=False)
    wd1 = inputs["L1_ffn_W_down"].astype(f32, copy=False)
    im = [{"H64": H64.astype(BF), "Wdn0": ct(wd0[c * OS:(c + 1) * OS]),
           "Wdn1": ct(wd1[c * OS:(c + 1) * OS])} for c in range(NCORES)]
    res = run_bass_kernel_spmd(nc_fold, im, core_ids=CORES)
    d2 = {}
    for li in range(2):
        D2 = np.concatenate([r[f"D2_{li}"] for r in res.results], axis=0)
        d2[li] = ct(D2.T.reshape(64, 128, D))

    nc_ln0 = build_ln(NT, False)
    res = run_bass_kernel_spmd(nc_ln0, [{"x": ct(s)} for s in np.split(xfull, NCORES)], core_ids=CORES)
    xn0 = np.concatenate([r["xn"] for r in res.results], axis=0)

    nc_b = build_l0proj()
    Wqkv = inputs["L0_attn_W_qkv"].astype(f32, copy=False)
    Wb_ = inputs["L0_attn_W_b"].astype(f32, copy=False)
    Wa_ = inputs["L0_attn_W_a"].astype(f32, copy=False)
    convw = inputs["L0_attn_conv_w"].astype(f32, copy=False)[:, 0, :]
    dtb = inputs["L0_attn_dt_bias"].astype(f32, copy=False)
    alog = inputs["L0_attn_A_log"].astype(f32, copy=False)
    im = []
    for c in range(NCORES):
        b, hh = c // 2, c % 2
        rows = list(range(hh * 256, (hh + 1) * 256)) + \
               list(range(512 + hh * 256, 512 + (hh + 1) * 256)) + \
               list(range(1024 + hh * 512, 1024 + (hh + 1) * 512))
        im.append({"xnf": ct(xn0[b * T:(b + 1) * T].T),
                   "wqkvT": ct(Wqkv[rows].T),
                   "wbT": ct(Wb_[hh * 8:(hh + 1) * 8].T),
                   "waT": ct(Wa_[hh * 8:(hh + 1) * 8].T),
                   "convw": ct(convw[rows]),
                   "dtb": ct(dtb[hh * 8:(hh + 1) * 8, None]),
                   "alog": ct(alog[hh * 8:(hh + 1) * 8, None]),
                   "ones2": ones2})
    res = run_bass_kernel_spmd(nc_b, im, core_ids=CORES)
    bouts = res.results

    nc_c = build_scan()
    im = []
    for c in range(NCORES):
        r = bouts[c]
        qkvf = r["qkvf"]
        im.append({"qf": ct(qkvf[0:256].reshape(2, 128, T)),
                   "kf": ct(qkvf[256:512].reshape(2, 128, T)),
                   "ktok": ct(qkvf[256:512].T), "vtok": ct(qkvf[512:1024].T),
                   "gcol": ct(r["grow"].T), "betacol": ct(r["betarow"].T),
                   "lnbcol": ct(r["lnbrow"].T), "lnrkcol": ct(r["lnrkrow"].T),
                   "lnrqcol": ct(r["lnrqrow"].T),
                   "UINC": UINC, "I128F": I128F, "I128B": I128B, "ONES1": ONES1})
    res = run_bass_kernel_spmd(nc_c, im, core_ids=CORES)
    o_glob = np.zeros((B, T, 16, 64), dtype=BF)
    for c in range(NCORES):
        b, hh = c // 2, c % 2
        o_glob[b, :, hh * 8:(hh + 1) * 8] = res.results[c]["o"].reshape(T, 8, 64)
    o_flat = o_glob.reshape(-1, 1024)

    nc_d = build_epi()
    Wz = inputs["L0_attn_W_z"].astype(f32, copy=False)
    Wout = inputs["L0_attn_W_out"].astype(f32, copy=False)
    nw = inputs["L0_attn_norm_weight"].astype(f32, copy=False)
    nw128 = ct(np.tile(nw, 2)[:, None])
    im = [{"o_in": ct(o_flat[c * NT:(c + 1) * NT]),
           "xnf": ct(xn0[c * NT:(c + 1) * NT].T),
           "wzT": ct(Wz.T), "woutT": ct(Wout.T), "nw128": nw128} for c in range(NCORES)]
    res = run_bass_kernel_spmd(nc_d, im, core_ids=CORES)
    attn0 = np.concatenate([ct(r["attnf"].T) for r in res.results], axis=0)

    nc_ffn = build_ffn()

    def run_ffn(xn, li):
        p = "L0_ffn_" if li == 0 else "L1_ffn_"
        im = [{"xnf": ct(xn[c * NT:(c + 1) * NT].T),
               "H2048": H2048c, "H128": H128c,
               "suug": ct(inputs[p + "SU_ug"].astype(f32, copy=False).reshape(16, 128).T),
               "sudp": ct(inputs[p + "SU_dp"].astype(f32, copy=False).reshape(64, 128).T),
               "wsug": ct(inputs[p + "Wscale_ug"].astype(f32, copy=False).reshape(128, 128).T),
               "wsdp": ct(inputs[p + "Wscale_dp"].astype(f32, copy=False).reshape(16, 128).T),
               "wupT": ct(inputs[p + "W_up"].astype(f32, copy=False).T),
               "wgT": ct(inputs[p + "W_gate"].astype(f32, copy=False).T),
               "d2T": d2[li]} for c in range(NCORES)]
        res = run_bass_kernel_spmd(nc_ffn, im, core_ids=CORES)
        return np.concatenate([ct(r["ffnf"].T) for r in res.results], axis=0)

    ffn0 = run_ffn(xn0, 0)

    nc_lnr = build_ln(NT, True)
    im = [{"x": ct(xfull[c * NT:(c + 1) * NT]), "a": ct(attn0[c * NT:(c + 1) * NT]),
           "f": ct(ffn0[c * NT:(c + 1) * NT])} for c in range(NCORES)]
    res = run_bass_kernel_spmd(nc_lnr, im, core_ids=CORES)
    x1 = np.concatenate([r["x1"] for r in res.results], axis=0)
    xn1 = np.concatenate([r["xn"] for r in res.results], axis=0)

    nc_g = build_qkv1()
    wq = ct(inputs["L1_attn_Wq"].astype(f32, copy=False).T)
    wk = ct(inputs["L1_attn_Wk"].astype(f32, copy=False).T)
    wv = ct(inputs["L1_attn_Wv"].astype(f32, copy=False).T)
    bq = ct(inputs["L1_attn_bq"].astype(f32, copy=False).reshape(16, 128).T)
    bk = ct(inputs["L1_attn_bk"].astype(f32, copy=False).reshape(16, 128).T)
    bv = ct(inputs["L1_attn_bv"].astype(f32, copy=False).reshape(16, 128).T)
    im = [{"xnf": ct(xn1[c * NT:(c + 1) * NT].T), "wqT": wq, "wkT": wk, "wvT": wv,
           "bq": bq, "bk": bk, "bv": bv} for c in range(NCORES)]
    res = run_bass_kernel_spmd(nc_g, im, core_ids=CORES)
    Qf = np.concatenate([r["Qf"] for r in res.results], axis=1)
    Kf = np.concatenate([r["Kf"] for r in res.results], axis=1)
    Vf = np.concatenate([r["Vf"] for r in res.results], axis=1)

    nc_h = build_attn()
    im = []
    for c in range(NCORES):
        b, hh = c // 2, c % 2
        rr = slice(hh * 1024, (hh + 1) * 1024)
        cc = slice(b * T, (b + 1) * T)
        im.append({"Qf": ct(Qf[rr, cc]), "Kf": ct(Kf[rr, cc]),
                   "Vtok": ct(Vf[rr, cc].T), "ONES1B": ONES1B, "ONESCB": ONESCB})
    res = run_bass_kernel_spmd(nc_h, im, core_ids=CORES)
    ctxf = np.zeros((2048, B * T), dtype=BF)
    for c in range(NCORES):
        b, hh = c // 2, c % 2
        ctxf[hh * 1024:(hh + 1) * 1024, b * T:(b + 1) * T] = res.results[c]["ctxf"]

    nc_i = build_wo()
    wo = ct(inputs["L1_attn_Wo"].astype(f32, copy=False).T)
    bo = ct(inputs["L1_attn_bo"].astype(f32, copy=False).reshape(16, 128).T)
    im = [{"xnf": ct(ctxf[:, c * NT:(c + 1) * NT]), "woT": wo, "bo": bo} for c in range(NCORES)]
    res = run_bass_kernel_spmd(nc_i, im, core_ids=CORES)
    attn1 = np.concatenate([ct(r["attnf"].T) for r in res.results], axis=0)

    ffn1 = run_ffn(xn1, 1)

    im = [{"x": ct(x1[c * NT:(c + 1) * NT]), "a": ct(attn1[c * NT:(c + 1) * NT]),
           "f": ct(ffn1[c * NT:(c + 1) * NT])} for c in range(NCORES)]
    res = run_bass_kernel_spmd(nc_lnr, im, core_ids=CORES)
    x2 = np.concatenate([r["x1"] for r in res.results], axis=0)
    return x2.reshape(B, T, D)


if __name__ == "__main__":
    pass
